# version 03 (final)
# baseline (speedup 1.0000x reference)
import numpy as np
from contextlib import ExitStack

import concourse.bass as bass
import concourse.mybir as mybir
from concourse.bass_utils import run_bass_kernel_spmd

F32 = mybir.dt.float32
BF16 = mybir.dt.bfloat16
U8 = mybir.dt.uint8
AF = mybir.ActivationFunctionType
OP = mybir.AluOpType
AX = mybir.AxisListType.X

T = 4096
D = 1024
NCORE = 8
CONST_VALS = [1e-5, -0.5, 1.0, 0.5, 2.0, 0.125]


def _bf16(a):
    import ml_dtypes
    return np.ascontiguousarray(np.asarray(a, dtype=np.float32).astype(ml_dtypes.bfloat16))


def _host_prep(inp):
    sh = {}
    sh["l0_wqkvT"] = np.ascontiguousarray(np.asarray(inp["L0_attn_w_qkv"]).T)
    sh["l0_bqkv"] = np.ascontiguousarray(np.asarray(inp["L0_attn_b_qkv"]).reshape(3072, 1))
    sh["l0_woutT"] = np.ascontiguousarray(np.asarray(inp["L0_attn_w_out"]).T)
    sh["l0_bout"] = np.ascontiguousarray(np.asarray(inp["L0_attn_b_out"]).reshape(1024, 1))
    sh["l0_gate"] = np.ascontiguousarray(np.asarray(inp["L0_ffn_gate_kernel"]))
    sh["l0_w0"] = np.ascontiguousarray(np.asarray(inp["L0_ffn_w0"]))
    sh["l0_w1"] = np.ascontiguousarray(np.asarray(inp["L0_ffn_w1"]))
    sh["l0_wo"] = np.ascontiguousarray(np.asarray(inp["L0_ffn_wo"]))
    p64 = np.empty(64, np.int64)
    p64[0:32] = 2 * np.arange(32)
    p64[32:64] = 2 * np.arange(32) + 1
    colperm_q = np.concatenate([64 * h + p64 for h in range(16)])
    sh["l1_wqa"] = np.ascontiguousarray(np.asarray(inp["L1_attn_Wqa"]))
    sh["l1_wqb"] = np.ascontiguousarray(np.asarray(inp["L1_attn_Wqb"])[:, colperm_q])
    sh["l1_wkva"] = np.ascontiguousarray(np.asarray(inp["L1_attn_Wkva"]))
    wkvb = np.asarray(inp["L1_attn_Wkvb"])
    kcols = np.concatenate([128 * h + 2 * p64 for h in range(4)])
    vcols = np.concatenate([128 * h + 2 * np.arange(64) + 1 for h in range(4)])
    sh["l1_wkvb_k"] = np.ascontiguousarray(wkvb[:, kcols])
    sh["l1_wkvb_v"] = np.ascontiguousarray(wkvb[:, vcols])
    sh["l1_wout"] = np.ascontiguousarray(np.asarray(inp["L1_attn_Wout"]))
    sh["l1_bout"] = np.ascontiguousarray(np.asarray(inp["L1_attn_bout"]).reshape(1024, 1))
    sh["lf_wqkv"] = np.ascontiguousarray(np.asarray(inp["L1_ffn_wqkv"]))
    sh["lf_qs"] = np.ascontiguousarray(np.asarray(inp["L1_ffn_q_scale"]).reshape(2, 64, 1))
    sh["lf_ks"] = np.ascontiguousarray(np.asarray(inp["L1_ffn_k_scale"]).reshape(2, 64, 1))
    sh["lf_wo"] = np.ascontiguousarray(np.asarray(inp["L1_ffn_wo"]))
    sh["lf_lng"] = np.ascontiguousarray(np.asarray(inp["L1_ffn_ln_g"]).reshape(2, 1024, 1))
    sh["lf_lnb"] = np.ascontiguousarray(np.asarray(inp["L1_ffn_ln_b"]).reshape(2, 1024, 1))
    sh["lf_w1"] = np.ascontiguousarray(np.asarray(inp["L1_ffn_w1"]))
    sh["lf_b1"] = np.ascontiguousarray(np.asarray(inp["L1_ffn_b1"]).reshape(2, 5460, 1))
    sh["lf_w2"] = np.ascontiguousarray(np.asarray(inp["L1_ffn_w2"]))
    sh["lf_b2"] = np.ascontiguousarray(np.asarray(inp["L1_ffn_b2"]).reshape(2, 1024, 1))
    ident = np.eye(128, dtype=np.float32)
    sh["c_ident_b"] = _bf16(ident)
    sh["c_ident_f"] = ident.copy()
    pm = np.zeros((128, 128), np.float32)
    for b in range(2):
        for i in range(32):
            pm[64 * b + i, 64 * b + 32 + i] = 1.0
            pm[64 * b + 32 + i, 64 * b + i] = 1.0
    sh["c_pm"] = _bf16(pm)
    sh["c_ones_b"] = _bf16(np.ones((128, 128), np.float32))
    sh["c_ones_f"] = np.ones((128, 128), np.float32)
    tri_le = np.ascontiguousarray(np.tril(np.ones((128, 128), np.float32)).T)
    sh["c_tri_le"] = _bf16(tri_le)
    sh["c_tri_ge"] = _bf16(np.ascontiguousarray(tri_le.T))
    t = np.arange(T, dtype=np.float64)
    f = np.arange(32, dtype=np.float64)
    inv = 1.0 / (10000.0 ** (2.0 * f / 64.0))
    ang = np.outer(inv, t)
    c32, s32 = np.cos(ang), np.sin(ang)
    sh["c_ropeC"] = _bf16(np.tile(np.concatenate([c32, c32], 0), (2, 1)))
    sh["c_ropeS"] = _bf16(np.tile(np.concatenate([-s32, s32], 0), (2, 1)))
    half = np.arange(0, 64, 2, dtype=np.float64)
    invf = 1.0 / (10000.0 ** (half / 64.0))
    fr64 = np.concatenate([np.outer(invf, t)] * 2, 0)
    base = (half + 0.4 * 64.0) / (1.4 * 64.0)
    power = (t - T // 2) / 256.0
    sc = base[:, None] ** power[None, :]
    sc64 = np.concatenate([sc, sc], 0)
    cosF, sinF = np.cos(fr64), np.sin(fr64)
    sgn = np.concatenate([-np.ones((32, 1)), np.ones((32, 1))], 0)
    sh["c_xposCQ"] = _bf16(np.tile(cosF * sc64, (2, 1)))
    sh["c_xposSQ"] = _bf16(np.tile(sgn * sinF * sc64, (2, 1)))
    sh["c_xposCK"] = _bf16(np.tile(cosF / sc64, (2, 1)))
    sh["c_xposSK"] = _bf16(np.tile(sgn * sinF / sc64, (2, 1)))
    sh["c_pool"] = np.asarray([CONST_VALS], np.float32)
    x = np.asarray(inp["L0_attn_x"])
    xs = [np.ascontiguousarray(x[i].T) for i in range(NCORE)]
    return sh, xs



ENGS = ("sync", "tensor", "vector", "scalar", "gpsimd")


class Prog:
    def __init__(self, nc, ctx):
        self.nc = nc
        self.q = {e: [] for e in ENGS}
        self.cnt = {}
        self.h = {}
        self.last_tok = {e: None for e in ENGS}
        self.esem = {"sync": "dS", "tensor": "T", "vector": "V",
                     "scalar": "S", "gpsimd": "dG"}
        for s in ("dS", "dG", "dO", "T", "V", "S", "dC", "dW0", "dW1", "dWo0", "dWo1", "dX", "dR", "dCW"):
            self.h[s] = ctx.enter_context(nc.semaphore(s))
            self.cnt[s] = 0
        self.last_w = {e: {} for e in ENGS}
        self.last_a = {e: {} for e in ENGS}
        self.fence = {e: -1 for e in ENGS}

    @staticmethod
    def _ap_name(a):
        t = getattr(a, "tensor", None)
        n = getattr(t, "name", None)
        return n

    def _rec(self, eng, meth, args, kwargs):
        tok = {"m": meth, "a": args, "k": kwargs, "inc": [], "pos": len(self.q[eng])}
        writes, reads = [], []
        for i, a in enumerate(args):
            n = self._ap_name(a)
            if n:
                (writes if i == 0 else reads).append(n)
        for k, a in kwargs.items():
            n = self._ap_name(a)
            if n:
                (writes if k in ("out", "accum_out", "dest") else reads).append(n)
        dep = None
        if eng == "tensor":
            tok2 = {"m": meth, "a": args, "k": kwargs, "inc": [],
                    "pos": len(self.q[eng])}
            self.q[eng].append(tok2)
            self.last_tok[eng] = tok2
            return tok2
        is_dma = meth == "dma_start"
        wmap, amap = self.last_w[eng], self.last_a[eng]
        for n in reads:
            t = wmap.get(n)
            if t is not None and t["pos"] > self.fence[eng] and not (
                    is_dma and t["m"] == "dma_start"):
                dep = t if dep is None or t["pos"] > dep["pos"] else dep
        for n in writes:
            t = amap.get(n)
            if t is not None and t["pos"] > self.fence[eng] and not (
                    is_dma and t["m"] == "dma_start"):
                dep = t if dep is None or t["pos"] > dep["pos"] else dep
        if dep is not None:
            if dep["inc"]:
                hh, _ = dep["inc"][0]
                s = next(nm for nm, hv in self.h.items() if hv is hh)
                v = dep["val_" + s]
            else:
                s = self.esem[eng]
                v = self.tick_tok(dep, s, self.step(eng))
            self.q[eng].append({"m": "wait_ge", "a": (self.h[s], v),
                                "k": {}, "inc": [], "pos": len(self.q[eng])})
            self.fence[eng] = dep["pos"]
        self.q[eng].append(tok)
        self.last_tok[eng] = tok
        if eng in ("vector", "scalar") and meth not in ("wait_ge",):
            self.tick_tok(tok, self.esem[eng], 1)
        for n in writes:
            wmap[n] = tok
            amap[n] = tok
        for n in reads:
            amap[n] = tok
        return tok

    def wait(self, eng, sem, val):
        if val <= 0:
            return
        self.q[eng].append({"m": "wait_ge", "a": (self.h[sem], val), "k": {},
                            "inc": [], "pos": len(self.q[eng])})

    def tick_tok(self, tok, sem, k=1):
        for ent in tok["inc"]:
            if ent[0] is self.h[sem]:
                ent[1] += k
                self.cnt[sem] += k
                tok["val_" + sem] = self.cnt[sem]
                return self.cnt[sem]
        tok["inc"].append([self.h[sem], k])
        self.cnt[sem] += k
        tok["val_" + sem] = self.cnt[sem]
        return self.cnt[sem]

    def e(self, eng):
        return _E(self, eng)

    def now(self, sem):
        return self.cnt[sem]

    def step(self, eng):
        return 16 if self.esem[eng] in ("dS", "dG") else 1

    def ss(self, eng):
        v = self.tick_tok(self.last_tok[eng], self.esem[eng], self.step(eng))
        self.wait(eng, self.esem[eng], v)

    def barrier(self, engines=ENGS):
        marks = {}
        for e in engines:
            tok = self.last_tok[e]
            if tok is None:
                continue
            if tok["inc"]:
                hh, _ = tok["inc"][0]
                s = next(nm for nm, hv in self.h.items() if hv is hh)
            else:
                s = self.esem[e]
                self.tick_tok(tok, s, self.step(e))
            marks[e] = (s, tok["val_" + s])
        for e in engines:
            for f_ in engines:
                if f_ != e and f_ in marks:
                    s, v = marks[f_]
                    self.wait(e, s, v)

    def run(self):
        with self.nc.Block() as block:
            for eng in ENGS:
                items = self.q[eng]

                def body(engine, items=items):
                    for it in items:
                        if it["m"] == "wait_ge":
                            engine.wait_ge(*it["a"])
                            continue
                        inst = getattr(engine, it["m"])(*it["a"], **it["k"])
                        for h, k in it["inc"]:
                            inst.then_inc(h, k)
                getattr(block, eng)(body)


class _E:
    def __init__(self, p, eng):
        self.p = p
        self.eng = eng

    def __getattr__(self, meth):
        def rec(*a, **kw):
            return self.p._rec(self.eng, meth, a, kw)
        return rec

    def wait(self, sem, val):
        self.p.wait(self.eng, sem, val)

    def tick(self, sem=None, k=None):
        sem = sem or self.p.esem[self.eng]
        if self.eng in ("vector", "scalar") and sem == self.p.esem[self.eng]:
            return self.p.cnt[sem]
        k = k or self.p.step(self.eng)
        return self.p.tick_tok(self.p.last_tok[self.eng], sem, k)

    def dma(self, out, in_, sem=None, **kw):
        self.p._rec(self.eng, "dma_start", (), dict(out=out, in_=in_, **kw))
        return self.tick(sem, 16)


class Arena:
    def __init__(self, nc, name, nbytes):
        self.nc = nc
        self.name = name
        self.nbytes = nbytes
        self.h = nc.alloc_sbuf_tensor(name, [128, nbytes], U8)
        self.base = nc.lookup_mloc(self.h).addr
        self.off = 0
        self.n = 0

    def reset(self):
        self.off = 0

    def alloc(self, shape, dtype):
        sz = shape[1] * mybir.dt.size(dtype)
        off = (self.off + 63) & ~63
        assert off + sz <= self.nbytes, f"{self.name} overflow {off}+{sz}>{self.nbytes}"
        self.off = off + sz
        self.n += 1
        h = self.nc.alloc_sbuf_tensor_at(f"{self.name}_v{self.n}", list(shape),
                                         dtype, offset=self.base + off)
        return h.ap()


class Bank:
    def __init__(self, ap):
        self.ap = ap
        self.w = None
        self.readers = []

    def begin_write(self, p):
        for s, v in self.readers:
            p.wait("tensor", s, v)
        self.readers = []

    def end_write(self, p, te):
        self.w = te.tick("T")

    def rd(self, p, e):
        if self.w:
            p.wait(e.eng if hasattr(e, "eng") else e, "T", self.w)

    def done_read(self, p, e):
        sem = p.esem[e.eng]
        v = e.tick(sem, p.step(e.eng))
        self.readers.append((sem, v))
        return v



DRAM_SPECS = [
    ("l0_wqkvT", (D, 3072), F32), ("l0_bqkv", (3072, 1), F32),
    ("l0_woutT", (D, D), F32), ("l0_bout", (D, 1), F32),
    ("l0_gate", (D, 8), F32), ("l0_w0", (8, D, 4096), F32),
    ("l0_w1", (8, D, 4096), F32), ("l0_wo", (8, 4096, D), F32),
    ("l1_wqa", (D, 64), F32), ("l1_wqb", (64, D), F32),
    ("l1_wkva", (D, 32), F32), ("l1_wkvb_k", (32, 256), F32),
    ("l1_wkvb_v", (32, 256), F32), ("l1_wout", (D, D), F32),
    ("l1_bout", (D, 1), F32),
    ("lf_wqkv", (2, D, 1536), F32), ("lf_qs", (2, 64, 1), F32),
    ("lf_ks", (2, 64, 1), F32), ("lf_wo", (2, 512, D), F32),
    ("lf_lng", (2, D, 1), F32), ("lf_lnb", (2, D, 1), F32),
    ("lf_w1", (2, D, 5460), F32), ("lf_b1", (2, 5460, 1), F32),
    ("lf_w2", (2, 2730, D), F32), ("lf_b2", (2, D, 1), F32),
    ("c_ident_b", (128, 128), BF16), ("c_ident_f", (128, 128), F32),
    ("c_pm", (128, 128), BF16), ("c_ones_b", (128, 128), BF16),
    ("c_ones_f", (128, 128), F32),
    ("c_tri_le", (128, 128), BF16), ("c_tri_ge", (128, 128), BF16),
    ("c_ropeC", (128, T), BF16), ("c_ropeS", (128, T), BF16),
    ("c_xposCQ", (128, T), BF16), ("c_xposSQ", (128, T), BF16),
    ("c_xposCK", (128, T), BF16), ("c_xposSK", (128, T), BF16),
    ("c_pool", (1, len(CONST_VALS)), F32),
]
FIXED_CONSTS = ["ident_b", "ident_f", "pm", "ones_b", "ones_f", "tri_le", "tri_ge"]


EMIT_LIMIT = None
_CK = [0]


class _StopEmit(Exception):
    pass


def _ck(tag=""):
    _CK[0] += 1
    if EMIT_LIMIT is not None and _CK[0] >= EMIT_LIMIT:
        raise _StopEmit(f"{tag}@{_CK[0]}")


def build_program(phases=("a", "b", "c", "d"), dump=(), feed=()):
    _CK[0] = 0
    nc = bass.Bass("TRN2", target_bir_lowering=False)
    ctx = ExitStack()
    io = {}

    def dram(name, shape, dtype, kind):
        if name in dump and kind == "Internal":
            kind = "ExternalOutput"
        if name in feed and kind == "Internal":
            kind = "ExternalInput"
        io[name] = nc.dram_tensor(name, list(shape), dtype, kind=kind).ap()

    dram("xT", (D, T), F32, "ExternalInput")
    for name, shape, dt_ in DRAM_SPECS:
        dram(name, shape, dt_, "ExternalInput")
    for name in ("r1T", "r2T", "r3T", "xiT"):
        dram(name, (D, T), F32, "Internal")
    dram("out", (T, D), F32, "ExternalOutput")

    fixed = {}
    for nm in FIXED_CONSTS:
        dt_ = F32 if nm.endswith("_f") else BF16
        fixed[nm] = nc.alloc_sbuf_tensor(nm, [128, 128], dt_).ap()
    fixed["pool"] = nc.alloc_sbuf_tensor("pool", [1, len(CONST_VALS) + 1], F32).ap()
    for i, v in enumerate(CONST_VALS):
        nc.const_aps.register(F32, float(v), fixed["pool"][0:1, i:i + 1])

    WA = Arena(nc, "WA", 65536)
    XA = Arena(nc, "XA", 73728)
    FA = Arena(nc, "FA", 53248)
    SM = Arena(nc, "SM", 4096)
    bias_stage = nc.alloc_sbuf_tensor("bias_stage", [128, 128], F32).ap()
    LNS = {
        "ab": nc.alloc_sbuf_tensor("ln_ab", [128, 2048], BF16).ap(),
        "sq": nc.alloc_sbuf_tensor("ln_sq", [128, 2048], BF16).ap(),
        "st": nc.alloc_sbuf_tensor("ln_st", [1, 1280], F32).ap(),
    }
    PS = ctx.enter_context(nc.psum_tensor("ps", [128, 7 * 512], F32)).ap()
    banks = [Bank(PS[:, 512 * i:512 * (i + 1)]) for i in range(7)]
    bankb = Bank(ctx.enter_context(nc.psum_tensor("pbT", [128, 1024], BF16)).ap())

    p = Prog(nc, ctx)
    sy = p.e("sync")
    for nm in FIXED_CONSTS:
        sy.dma(fixed[nm][:, :], io["c_" + nm], sem="dC")
    ld0 = sy.dma(fixed["pool"][0:1, 0:len(CONST_VALS)], io["c_pool"], sem="dC")
    for e in ENGS:
        if e != "sync":
            p.wait(e, "dC", ld0)

    B = dict(io=io, fixed=fixed, WA=WA, XA=XA, FA=FA, SM=SM, LNS=LNS,
             banks=banks, bankb=bankb, bias_stage=bias_stage, PS=PS)

    for ph in ("a", "b", "c", "d"):
        if ph in phases:
            try:
                globals()["emit_phase_" + ph](p, B)
            except _StopEmit as e:
                print("EMIT STOPPED:", e)
            p.barrier()
            WA.reset(); XA.reset(); FA.reset(); SM.reset()

    p.wait("sync", "dO", p.now("dO"))
    p.run()
    ctx.close()
    nc._B = B
    return nc



def ident_at(B, rbase, n):
    return B["fixed"]["ident_b"][rbase:rbase + n, rbase:rbase + n]


def load_bias(p, B, src, n, dest, bank):
    te, ve, sy = p.e("tensor"), p.e("vector"), p.e("sync")
    fixed = B["fixed"]
    nfull = n // 128
    rem = n - 128 * nfull
    stage = B["bias_stage"]
    p.wait("sync", "T", p.now("T"))
    lv = 0
    if nfull:
        lv = sy.dma(stage[0:nfull, 0:128],
                    src[0:128 * nfull, 0:1].rearrange("(c p) o -> c (p o)", p=128),
                    sem="dR")
    if rem:
        lv = sy.dma(stage[nfull:nfull + 1, 0:rem],
                    src.rearrange("(a r) o -> a (r o)", a=1)[0:1, 128 * nfull:n],
                    sem="dR")
    p.wait("tensor", "dR", lv)
    m = nfull + (1 if rem else 0)
    bank.begin_write(p)
    te.transpose(bank.ap[0:128, 0:m], stage[0:m, 0:128],
                 B["fixed"]["ident_f"][0:m, 0:m])
    bank.end_write(p, te)
    bank.rd(p, ve)
    ve.tensor_copy(dest[:, 0:m], bank.ap[0:128, 0:m])
    v = bank.done_read(p, ve)
    return v


def emit_ln(p, B, acc, W, dst_hbm, tcol, bf_out=None, lg=None, lb=None):
    te, ve, se, sy = p.e("tensor"), p.e("vector"), p.e("scalar"), p.e("sync")
    fixed, banks, LNS = B["fixed"], B["banks"], B["LNS"]
    ab, sq, st = LNS["ab"], LNS["sq"], LNS["st"]
    row = lambda j: st[0:1, j * W:(j + 1) * W]
    bsum, bbc1, bbc2 = banks[4], banks[5], banks[6]

    p.wait("vector", "T", p.now("T"))
    for c in range(8):
        ve.tensor_copy(ab[:, c * W:(c + 1) * W], acc[c])
    vv = ve.tick()
    p.wait("scalar", "V", vv)
    se.activation(sq[:, 0:8 * W], ab[:, 0:8 * W], AF.Square)
    sv = se.tick()
    p.wait("tensor", "V", vv)
    p.wait("tensor", "S", sv)
    bsum.begin_write(p)
    for c in range(8):
        te.matmul(bsum.ap[0:1, 0:W], fixed["ones_b"][:, 0:1],
                  ab[:, c * W:(c + 1) * W], start=(c == 0), stop=False)
        te.matmul(bsum.ap[0:1, W:2 * W], fixed["ones_b"][:, 0:1],
                  sq[:, c * W:(c + 1) * W], start=False, stop=(c == 7))
    bsum.end_write(p, te)
    bsum.rd(p, ve)
    ve.tensor_scalar_mul(row(0), bsum.ap[0:1, 0:W], 1.0 / D)
    ve.tensor_scalar_mul(row(1), bsum.ap[0:1, W:2 * W], 1.0 / D)
    bsum.done_read(p, ve)
    ve.tensor_mul(row(4), row(0), row(0))
    p.ss("vector")
    ve.tensor_sub(row(1), row(1), row(4))
    vv = ve.tick()
    p.wait("scalar", "V", vv)
    se.activation(row(4), row(1), AF.Ln, bias=1e-5)
    p.ss("scalar")
    se.activation(row(2), row(4), AF.Exp, scale=-0.5)
    sv = se.tick()
    p.wait("vector", "S", sv)
    ve.scalar_tensor_tensor(row(3), row(0), -1.0, row(2), OP.mult, OP.mult)
    vv = ve.tick()
    p.wait("tensor", "V", vv)
    bbc1.begin_write(p)
    te.matmul(bbc1.ap[:, 0:W], fixed["ones_f"][0:1, :], row(2), start=True, stop=True)
    bbc1.end_write(p, te)
    bbc2.begin_write(p)
    te.matmul(bbc2.ap[:, 0:W], fixed["ones_f"][0:1, :], row(3), start=True, stop=True)
    bbc2.end_write(p, te)
    bbc1.rd(p, ve)
    bbc2.rd(p, ve)
    for c in range(8):
        ve.tensor_mul(acc[c], acc[c], bbc1.ap[:, 0:W])
    for c in range(8):
        ve.tensor_add(acc[c], acc[c], bbc2.ap[:, 0:W])
    if lg is not None:
        for c in range(8):
            ve.tensor_scalar(acc[c], acc[c], lg[c], lb[c], OP.mult, OP.add)
    if bf_out is not None:
        for c in range(8):
            ve.tensor_copy(bf_out[c], acc[c])
    bbc1.done_read(p, ve)
    bbc2.done_read(p, ve)
    vv = ve.tick()
    dv = None
    if dst_hbm is not None:
        p.wait("sync", "V", vv)
        for c in range(8):
            dv = sy.dma(dst_hbm[128 * c:128 * (c + 1), tcol:tcol + W], acc[c])
    return vv, dv


def load_xchunk(p, B, hbm, buf, tcol, W, eng, sem):
    e = p.e(eng)
    return e.dma(buf.rearrange("p (c w) -> p c w", w=W),
                 hbm.rearrange("(c p) t -> p c t", p=128)[:, :, tcol:tcol + W],
                 sem=sem)



def emit_phase_a(p, B):
    io, fixed, banks, bankb = B["io"], B["fixed"], B["banks"], B["bankb"]
    WA, XA, FA, SM = B["WA"], B["XA"], B["FA"], B["SM"]
    te, ve, se, sy, ge = (p.e("tensor"), p.e("vector"), p.e("scalar"),
                          p.e("sync"), p.e("gpsimd"))

    WQKV = WA.alloc((128, 8 * 3072), BF16)
    WOUT = WA.alloc((128, 8 * 1024), BF16)
    XC = XA.alloc((128, 8 * 512), BF16)
    QKV = XA.alloc((128, 24 * 512), BF16)
    GB = XA.alloc((128, 24 * 256), BF16)
    VT = XA.alloc((128, 4 * 16 * 64), BF16)
    VG = XA.alloc((128, 2 * 16 * 64), BF16)
    PT = XA.alloc((128, 384), BF16)
    PG = XA.alloc((128, 1024), BF16)
    OB = XA.alloc((128, 4 * 1024), BF16)
    ACC = FA.alloc((128, 8 * 128), F32)
    OG = FA.alloc((128, 8 * 256), F32)
    OGZ = FA.alloc((1, 16 * 256), F32)
    XR = FA.alloc((128, 8 * 128), F32)
    OT = FA.alloc((128, 128), F32)
    RZ = SM.alloc((1, 128), F32)
    RZB = SM.alloc((1, 128), BF16)
    BQ = SM.alloc((128, 24), F32)
    BO = SM.alloc((128, 8), F32)
    st = dict(pt_free=0, pg_free=0, acc_free=0)
    B['dbg_a'] = dict(QKV=QKV, GB=GB, VT=VT, OB=OB, ACC=ACC, OG=OG, OGZ=OGZ)

    for c in range(8):
        ge.dma(WQKV[:, 3072 * c:3072 * (c + 1)],
               io["l0_wqkvT"][128 * c:128 * (c + 1), :], sem="dCW")
        wload = ge.dma(WOUT[:, 1024 * c:1024 * (c + 1)],
                       io["l0_woutT"][128 * c:128 * (c + 1), :], sem="dCW")
    load_bias(p, B, io["l0_bqkv"], 3072, BQ, banks[6])
    bv = load_bias(p, B, io["l0_bout"], 1024, BO, banks[6])
    p.wait("scalar", "V", bv)

    ve.memset(OG[:, :], 0.0)
    ve.memset(OGZ[0:1, :], 0.0)
    ve.tick()
    _ck("a:loads")

    def qkv_for_tc(tc, stash):
        t0 = tc * 512
        p.wait("gpsimd", "T", p.now("T"))
        lv = load_xchunk(p, B, io["xT"], XC, t0, 512, "gpsimd", "dX")
        p.wait("tensor", "dX", lv)
        p.wait("tensor", "dCW", wload)
        p.wait("scalar", "T", p.now("T"))
        p.wait("scalar", "V", p.now("V"))
        _ck(f"a:qkv:ld")
        for j in range(24):
            _ck(f"a:qkv:j{j}")
            bk = banks[j % 2]
            bk.begin_write(p)
            for c in range(8):
                te.matmul(bk.ap[:, 0:512],
                          WQKV[:, 3072 * c + 128 * j:3072 * c + 128 * (j + 1)],
                          XC[:, 512 * c:512 * (c + 1)],
                          start=(c == 0), stop=(c == 7))
            bk.end_write(p, te)
            bk.rd(p, se)
            se.activation(QKV[:, 512 * j:512 * (j + 1)], bk.ap[:, 0:512],
                          AF.Identity, bias=BQ[:, j:j + 1])
            bk.done_read(p, se)
        sv = p.now("S")
        _ck("a:qkv:post")
        if stash:
            gofs = 0 if tc == 0 else 128
            p.wait("vector", "S", sv)
            for j in range(24):
                ve.tensor_copy(GB[:, 256 * j + gofs:256 * j + gofs + 128],
                               QKV[:, 512 * j:512 * j + 128])
            ve.tick()
            _ck("a:qkv:stashcp")
        return sv

    for tc in (0, 6):
        qkv_for_tc(tc, stash=True)
    p.wait("tensor", "V", p.now("V"))
    for gc in range(2):
        bankb.begin_write(p)
        for m in range(8):
            vcol = (16 + m) * 256 + gc * 128
            te.transpose(bankb.ap[:, 128 * m:128 * (m + 1)],
                         GB[0:128, vcol:vcol + 128],
                         B["fixed"]["ident_b"][0:128, 0:128])
        bankb.end_write(p, te)
        bankb.rd(p, ve)
        ve.tensor_copy(VG[:, 1024 * gc:1024 * (gc + 1)], bankb.ap[:, 0:1024])
        bankb.done_read(p, ve)
        _ck(f"a:vg{gc}")
    _ck("a:stash")

    def attn_norm(h, bo_bank, src_cols, out_cols, src_is_og):
        rbase = (64 * h) % 128
        bo_bank.rd(p, ve)
        p.wait("vector", "T", p.now("T"))
        if src_is_og:
            ve.reciprocal(RZ[0:1, 0:128], OGZ[0:1, src_cols:src_cols + 128])
            p.ss("vector")
        else:
            ve.reciprocal(RZ[0:1, 0:128], bo_bank.ap[0:1, 128:256])
            ve.tensor_copy(OT[rbase:rbase + 64, 0:128], bo_bank.ap[0:64, 0:128])
        ve.tensor_copy(RZB[0:1, 0:128], RZ[0:1, 0:128])
        rv = bo_bank.done_read(p, ve)
        p.wait("tensor", "V", rv)
        bz = banks[4]
        bz.begin_write(p)
        te.matmul(bz.ap[0:64, 0:128], fixed["ones_b"][0:1, 0:64],
                  RZB[0:1, 0:128], start=True, stop=True)
        bz.end_write(p, te)
        bz.rd(p, ve)
        if src_is_og:
            gi = (src_cols % 256) // 128
            gofs = (h // 2) * 256 + 128 * gi
            ve.tensor_mul(OB[rbase:rbase + 64, out_cols:out_cols + 128],
                          OG[rbase:rbase + 64, gofs:gofs + 128],
                          bz.ap[0:64, 0:128])
        else:
            ve.tensor_mul(OB[rbase:rbase + 64, out_cols:out_cols + 128],
                          OT[rbase:rbase + 64, 0:128], bz.ap[0:64, 0:128])
        bz.done_read(p, ve)

    def proj_ln_block(blk, tglob, obv):
        p.wait("sync", "V", p.now("V"))
        lv = load_xchunk(p, B, io["xT"], XR, tglob, 128, "sync", "dR")
        p.wait("tensor", "V", obv)
        accs = [ACC[:, 128 * j:128 * (j + 1)] for j in range(8)]
        for j in range(8):
            bk = banks[j % 2]
            bk.begin_write(p)
            for c in range(8):
                te.matmul(bk.ap[:, 0:128],
                          WOUT[:, 1024 * c + 128 * j:1024 * c + 128 * (j + 1)],
                          OB[:, blk * 1024 + 128 * c:blk * 1024 + 128 * (c + 1)],
                          start=(c == 0), stop=(c == 7))
            bk.end_write(p, te)
            bk.rd(p, ve)
            if j == 0:
                p.wait("vector", "dR", lv)
                p.wait("vector", "dS", st["acc_free"])
            ve.scalar_tensor_tensor(accs[j], bk.ap[:, 0:128], BO[:, j:j + 1],
                                    XR[:, 128 * j:128 * (j + 1)], OP.add, OP.add)
            bk.done_read(p, ve)
        ve.tick()
        vv, dv = emit_ln(p, B, accs, 128, io["r1T"], tglob)
        st["acc_free"] = dv

    for tc in range(8):
        t0 = tc * 512
        sv = qkv_for_tc(tc, stash=False)
        p.wait("tensor", "S", sv)
        p.wait("vector", "T", p.now("T"))
        for blk in range(4):
            bankb.begin_write(p)
            for m in range(8):
                vcol = (16 + m) * 512 + 128 * blk
                te.transpose(bankb.ap[:, 128 * m:128 * (m + 1)],
                             QKV[0:128, vcol:vcol + 128],
                             B["fixed"]["ident_b"][0:128, 0:128])
            bankb.end_write(p, te)
            bankb.rd(p, ve)
            ve.tensor_copy(VT[:, 1024 * blk:1024 * (blk + 1)], bankb.ap[:, 0:1024])
            bankb.done_read(p, ve)
        vtv = ve.tick()
        p.wait("tensor", "V", vtv)
        _ck(f"a:tc{tc}:vt")

        for blk in range(4):
            if (tc == 0 or tc == 6) and blk == 0:
                continue
            for h in range(16):
                rbase = (64 * h) % 128
                kcol = (8 + h // 2) * 512 + 128 * blk
                qcol = (h // 2) * 512 + 128 * blk
                bs = banks[h % 2]
                bs.begin_write(p)
                te.matmul(bs.ap[:, 0:128], QKV[rbase:rbase + 64, kcol:kcol + 128],
                          QKV[rbase:rbase + 64, qcol:qcol + 128],
                          start=True, stop=False)
                for gc in range(2):
                    gcol = (8 + h // 2) * 256 + 128 * gc
                    te.matmul(bs.ap[:, 128 * (1 + gc):128 * (2 + gc)],
                              GB[rbase:rbase + 64, gcol:gcol + 128],
                              QKV[rbase:rbase + 64, qcol:qcol + 128],
                              start=False, stop=(gc == 1))
                bs.end_write(p, te)
                bs.rd(p, se)
                p.wait("scalar", "T", st["pt_free"])
                se.activation(PT[:, 0:384], bs.ap[:, 0:384], AF.Exp, scale=0.125)
                ev = bs.done_read(p, se)
                p.wait("tensor", "S", ev)
                bo = banks[2 + h % 2]
                bo.begin_write(p)
                te.matmul(bo.ap[0:64, 0:128],
                          VT[:, 1024 * blk + 64 * h:1024 * blk + 64 * (h + 1)],
                          PT[:, 0:128], start=True, stop=False)
                te.matmul(bo.ap[0:1, 128:256], fixed["ones_b"][:, 0:1],
                          PT[:, 0:128], start=False, stop=False)
                for gc in range(2):
                    te.matmul(bo.ap[0:1, 128:256], fixed["ones_b"][:, 0:1],
                              PT[:, 128 * (1 + gc):128 * (2 + gc)],
                              start=False, stop=False)
                    te.matmul(bo.ap[0:64, 0:128],
                              VG[:, 1024 * gc + 64 * h:1024 * gc + 64 * (h + 1)],
                              PT[:, 128 * (1 + gc):128 * (2 + gc)],
                              start=False, stop=(gc == 1))
                bo.end_write(p, te)
                st["pt_free"] = bo.w
                attn_norm(h, bo, None, blk * 1024 + (h // 2) * 128, False)
            _ck(f"a:tc{tc}:blk{blk}")

        for h in range(16):
            rbase = (64 * h) % 128
            for half in range(2):
                bs = banks[half]
                bs.begin_write(p)
                for kc in range(2):
                    kcol = (8 + h // 2) * 512 + 128 * (2 * half + kc)
                    te.matmul(bs.ap[:, 256 * kc:256 * (kc + 1)],
                              QKV[rbase:rbase + 64, kcol:kcol + 128],
                              GB[rbase:rbase + 64, (h // 2) * 256:(h // 2) * 256 + 256],
                              start=(kc == 0), stop=(kc == 1))
                bs.end_write(p, te)
                bs.rd(p, se)
                if half == 0:
                    p.wait("scalar", "T", st["pg_free"])
                se.activation(PG[:, 512 * half:512 * (half + 1)],
                              bs.ap[:, 0:512], AF.Exp, scale=0.125)
                bs.done_read(p, se)
            ev = p.now("S")
            p.wait("tensor", "S", ev)
            bo = banks[2]
            bo.begin_write(p)
            for kc in range(4):
                if kc > 0:
                    te.matmul(bo.ap[0:1, 256:512], fixed["ones_b"][:, 0:1],
                              PG[:, 256 * kc:256 * (kc + 1)], start=False, stop=False)
                te.matmul(bo.ap[0:64, 0:256],
                          VT[:, 1024 * kc + 64 * h:1024 * kc + 64 * (h + 1)],
                          PG[:, 256 * kc:256 * (kc + 1)], start=(kc == 0),
                          stop=(kc == 3))
                if kc == 0:
                    te.matmul(bo.ap[0:1, 256:512], fixed["ones_b"][:, 0:1],
                              PG[:, 256 * kc:256 * (kc + 1)], start=False, stop=False)
            bo.end_write(p, te)
            st["pg_free"] = bo.w
            bo.rd(p, ve)
            gcols = (h // 2) * 256
            ve.tensor_add(OG[rbase:rbase + 64, gcols:gcols + 256],
                          OG[rbase:rbase + 64, gcols:gcols + 256],
                          bo.ap[0:64, 0:256])
            ve.tensor_add(OGZ[0:1, 256 * h:256 * (h + 1)],
                          OGZ[0:1, 256 * h:256 * (h + 1)], bo.ap[0:1, 256:512])
            bo.done_read(p, ve)
        _ck(f"a:tc{tc}:gq")

        obv = p.now("V")
        for blk in range(4):
            if (tc == 0 or tc == 6) and blk == 0:
                continue
            proj_ln_block(blk, t0 + 128 * blk, obv)
            _ck(f"a:tc{tc}:ln{blk}")

    for gi, gt in enumerate((0, 3072)):
        for h in range(16):
            bo = banks[2]
            attn_norm(h, bo, 256 * h + 128 * gi, (h // 2) * 128, True)
        obv = p.now("V")
        proj_ln_block(0, gt, obv)
        _ck(f"a:fin{gi}")


def emit_phase_b(p, B):
    io, fixed, banks, bankb = B["io"], B["fixed"], B["banks"], B["bankb"]
    WA, XA, FA, SM = B["WA"], B["XA"], B["FA"], B["SM"]
    te, ve, se, sy, ge = (p.e("tensor"), p.e("vector"), p.e("scalar"),
                          p.e("sync"), p.e("gpsimd"))

    G = XA.alloc((128, 32 * 1024), BF16)
    W0C = [WA.alloc((128, 8 * 512), BF16) for _ in range(2)]
    W1C = [WA.alloc((128, 8 * 512), BF16) for _ in range(2)]
    WOC = [WA.alloc((128, 32 * 128), BF16) for _ in range(2)]
    GS = [WA.alloc((128, 512), BF16) for _ in range(2)]
    GK = WA.alloc((128, 64), BF16)
    ACC = FA.alloc((128, 8 * 1024), F32)
    XTC = FA.alloc((128, 8 * 1024), BF16)
    CBS = [FA.alloc((128, 512), F32), WA.alloc((128, 512), F32)]
    TMP = WA.alloc((128, 512), F32)
    R1C = WA.alloc((128, 8 * 256), F32)
    CB = SM.alloc((128, 64), F32)
    CTC = SM.alloc((1, 1024), BF16)
    LG = SM.alloc((128, 8), F32)
    TOP = SM.alloc((128, 8), F32)
    MSK = SM.alloc((128, 24), F32)
    WT12 = SM.alloc((128, 4), F32)

    for c in range(8):
        ge.dma(GK[:, 8 * c:8 * (c + 1)],
               io["l0_gate"][128 * c:128 * (c + 1), :], sem="dCW")
    gkload = p.now("dCW")
    st = dict(w_free=[0, 0], wo_free=[0, 0], acc_st=0, gs_free=[0, 0])
    B['dbg_b'] = dict(G=G, ACC=ACC, XTC=XTC, CB=CB, CTC=CTC, CBS=CBS, LG=LG, TOP=TOP, MSK=MSK, WT12=WT12, GK=GK, W0C=W0C, WOC=WOC)

    def load_wchunk(e, fb, par):
        p.wait("gpsimd", "T", st["w_free"][par])
        s = "dW%d" % par
        v = ge.dma(W0C[par].rearrange("p (c f) -> p c f", f=512),
                   io["l0_w0"][e].rearrange("(c p) f -> p c f", p=128)[:, :, 512 * fb:512 * (fb + 1)],
                   sem=s)
        v = ge.dma(W1C[par].rearrange("p (c f) -> p c f", f=512),
                   io["l0_w1"][e].rearrange("(c p) f -> p c f", p=128)[:, :, 512 * fb:512 * (fb + 1)],
                   sem=s)
        return (s, v)

    def load_wo(e, j, par):
        p.wait("gpsimd", "T", st["wo_free"][par])
        s = "dWo%d" % par
        return (s, ge.dma(WOC[par].rearrange("p (c d) -> p c d", d=128),
                          io["l0_wo"][e].rearrange("(c p) d -> p c d", p=128)[:, :, 128 * j:128 * (j + 1)],
                          sem=s))

    for tc in range(4):
        t0 = 1024 * tc
        p.wait("gpsimd", "T", p.now("T"))
        xv = load_xchunk(p, B, io["r1T"], XTC, t0, 1024, "gpsimd", "dX")
        p.wait("tensor", "dX", xv)
        p.wait("tensor", "dCW", gkload)
        for tt in range(8):
            bk = banks[4]
            bk.begin_write(p)
            for c in range(8):
                te.matmul(bk.ap[0:128, 0:8], XTC[:, 1024 * c + 128 * tt:1024 * c + 128 * (tt + 1)],
                          GK[:, 8 * c:8 * (c + 1)], start=(c == 0), stop=(c == 7))
            bk.end_write(p, te)
            bk.rd(p, ve)
            ve.tensor_copy(LG[:, :], bk.ap[0:128, 0:8])
            bk.done_read(p, ve)
            ve.max(TOP[:, :], LG[:, :])
            ve.tensor_sub(WT12[:, 0:1], TOP[:, 1:2], TOP[:, 0:1])
            vv = ve.tick()
            p.wait("scalar", "V", vv)
            se.activation(WT12[:, 1:2], WT12[:, 0:1], AF.Exp)
            sv = se.tick()
            p.wait("vector", "S", sv)
            ve.tensor_scalar_add(WT12[:, 2:3], WT12[:, 1:2], 1.0)
            ve.reciprocal(WT12[:, 2:3], WT12[:, 2:3])
            ve.tensor_mul(WT12[:, 3:4], WT12[:, 1:2], WT12[:, 2:3])
            ve.tensor_scalar(MSK[:, 0:8], LG[:, :], TOP[:, 0:1], None, OP.is_equal)
            ve.tensor_scalar(MSK[:, 8:16], LG[:, :], TOP[:, 1:2], None, OP.is_equal)
            ve.tensor_scalar(MSK[:, 16:24], MSK[:, 0:8], WT12[:, 2:3], None, OP.mult)
            ve.scalar_tensor_tensor(CB[:, 8 * tt:8 * (tt + 1)], MSK[:, 8:16],
                                    WT12[:, 3:4], MSK[:, 16:24], OP.mult, OP.add)

        for e in range(8):
            p.wait("vector", "T", p.now("T"))
            wv = load_wchunk(e, 0, 0)
            for fb in range(8):
                par = fb % 2
                if fb < 7:
                    nwv = load_wchunk(e, fb + 1, 1 - par)
                p.wait("tensor", wv[0], wv[1])
                for ft in range(4):
                    fi = 4 * fb + ft
                    for th in range(2):
                        b1 = banks[th]
                        b1.begin_write(p)
                        for c in range(8):
                            te.matmul(b1.ap[:, 0:512],
                                      W0C[par][:, 512 * c + 128 * ft:512 * c + 128 * (ft + 1)]
                                      if False else
                                      W0C[par][:, 512 * c + 128 * ft:512 * c + 128 * ft + 128],
                                      XTC[:, 1024 * c + 512 * th:1024 * c + 512 * (th + 1)],
                                      start=(c == 0), stop=(c == 7))
                        b1.end_write(p, te)
                        b2 = banks[2 + th]
                        b2.begin_write(p)
                        for c in range(8):
                            te.matmul(b2.ap[:, 0:512],
                                      W1C[par][:, 512 * c + 128 * ft:512 * c + 128 * ft + 128],
                                      XTC[:, 1024 * c + 512 * th:1024 * c + 512 * (th + 1)],
                                      start=(c == 0), stop=(c == 7))
                        b2.end_write(p, te)
                        gpar = (2 * fi + th) % 2
                        b1.rd(p, se)
                        p.wait("scalar", "V", st["gs_free"][gpar])
                        se.activation(GS[gpar][:, :], b1.ap[:, 0:512], AF.Silu)
                        b1.done_read(p, se)
                        sv = se.tick()
                        p.wait("vector", "S", sv)
                        b2.rd(p, ve)
                        ve.tensor_mul(G[:, 1024 * fi + 512 * th:1024 * fi + 512 * (th + 1)],
                                      GS[gpar][:, :], b2.ap[:, 0:512])
                        b2.done_read(p, ve)
                        st["gs_free"][gpar] = p.now("V")
                st["w_free"][par] = p.now("T")
                if fb < 7:
                    wv = nwv
            gv = p.now("V")
            p.wait("tensor", "V", p.now("V"))
            for half in range(2):
                cb_bank = banks[4 + half]
                cb_bank.begin_write(p)
                for q in range(4):
                    tt = 4 * half + q
                    te.transpose(cb_bank.ap[0:1, 128 * q:128 * (q + 1)],
                                 CB[:, 8 * tt + e:8 * tt + e + 1],
                                 fixed["ident_f"][:, :])
                cb_bank.end_write(p, te)
                cb_bank.rd(p, ve)
                ve.tensor_copy(CTC[0:1, 512 * half:512 * (half + 1)],
                               cb_bank.ap[0:1, 0:512])
                cb_bank.done_read(p, ve)
            ctv = p.now("V")
            p.wait("tensor", "V", ctv)
            for th in range(2):
                bk = banks[4]
                bk.begin_write(p)
                te.matmul(bk.ap[:, 0:512], fixed["ones_b"][0:1, :],
                          CTC[0:1, 512 * th:512 * (th + 1)], start=True, stop=True)
                bk.end_write(p, te)
                bk.rd(p, ve)
                ve.tensor_copy(CBS[th][:, :], bk.ap[:, 0:512])
                bk.done_read(p, ve)
            gv = p.now("V")
            wov = load_wo(e, 0, 0)
            for j in range(8):
                par = j % 2
                if j < 7:
                    nwov = load_wo(e, j + 1, 1 - par)
                p.wait("tensor", wov[0], wov[1])
                if j == 0:
                    p.wait("tensor", "V", gv)
                for th in range(2):
                    bw = banks[5 + th]
                    bw.begin_write(p)
                    for fb in range(32):
                        te.matmul(bw.ap[:, 0:512],
                                  WOC[par][:, 128 * fb:128 * (fb + 1)],
                                  G[:, 1024 * fb + 512 * th:1024 * fb + 512 * (th + 1)],
                                  start=(fb == 0), stop=(fb == 31))
                    bw.end_write(p, te)
                    bw.rd(p, ve)
                    if e == 0:
                        if j == 0 and th == 0:
                            p.wait("vector", "dS", st["acc_st"])
                        ve.scalar_tensor_tensor(
                            ACC[:, 1024 * j + 512 * th:1024 * j + 512 * (th + 1)],
                            bw.ap[:, 0:512], 1.0, CBS[th][:, :], OP.mult, OP.mult)
                    else:
                        ve.tensor_mul(TMP[:, :], bw.ap[:, 0:512], CBS[th][:, :])
                        ve.tensor_add(ACC[:, 1024 * j + 512 * th:1024 * j + 512 * (th + 1)],
                                      ACC[:, 1024 * j + 512 * th:1024 * j + 512 * (th + 1)],
                                      TMP[:, :])
                    bw.done_read(p, ve)
                st["wo_free"][par] = p.now("T")
                if j < 7:
                    wov = nwov
        for w in range(4):
            tcol = t0 + 256 * w
            p.wait("sync", "V", p.now("V"))
            lv = load_xchunk(p, B, io["r1T"], R1C, tcol, 256, "sync", "dR")
            p.wait("vector", "dR", lv)
            accs = []
            for c in range(8):
                a = ACC[:, 1024 * c + 256 * w:1024 * c + 256 * (w + 1)]
                ve.tensor_add(a, a, R1C[:, 256 * c:256 * (c + 1)])
                accs.append(a)
            vv, dv = emit_ln(p, B, accs, 256, io["r2T"], tcol)
        st["acc_st"] = p.now("dS")


def emit_phase_c(p, B):
    io, fixed, banks, bankb = B["io"], B["fixed"], B["banks"], B["bankb"]
    WA, XA, FA, SM = B["WA"], B["XA"], B["FA"], B["SM"]
    te, ve, se, sy, ge = (p.e("tensor"), p.e("vector"), p.e("scalar"),
                          p.e("sync"), p.e("gpsimd"))

    KR = XA.alloc((128, 2 * 4096), BF16)
    VRQH = XA.alloc((128, 8192), BF16)
    VR = VRQH
    QH = VRQH[:, 0:4096]
    QHs = VRQH[:, 4096:8192]
    VT = XA.alloc((128, 4 * 32 * 65), BF16)
    XTC = XA.alloc((128, 8 * 512), BF16)
    PC = [XA.alloc((128, 512), BF16) for _ in range(2)]
    OBUF = XA.alloc((128, 8 * 512), BF16)
    USB = WA.alloc((64, 512), BF16)
    KVU = WA.alloc((32, 4096), BF16)
    WOUT = WA.alloc((128, 8 * 1024), BF16)
    CS_C = WA.alloc((128, 4096), BF16)
    CS_S = WA.alloc((128, 4096), BF16)
    WQA = WA.alloc((128, 8 * 64), BF16)
    WQB = WA.alloc((64, 1024), BF16)
    WKVA = WA.alloc((128, 8 * 32), BF16)
    WKVB_K = WA.alloc((32, 256), BF16)
    WKVB_V = WA.alloc((32, 256), BF16)
    TMP1 = WA.alloc((128, 512), F32)
    TMP2 = WA.alloc((128, 512), F32)
    ACC = FA.alloc((128, 8 * 256), F32)
    R2C = FA.alloc((128, 8 * 256), F32)
    OT = FA.alloc((128, 512), F32)
    RZ = SM.alloc((1, 512), F32)
    RZB = SM.alloc((1, 512), BF16)
    BO = SM.alloc((128, 8), F32)

    for c in range(8):
        ge.dma(WQA[:, 64 * c:64 * (c + 1)], io["l1_wqa"][128 * c:128 * (c + 1), :], sem="dCW")
        ge.dma(WKVA[:, 32 * c:32 * (c + 1)], io["l1_wkva"][128 * c:128 * (c + 1), :], sem="dCW")
        ge.dma(WOUT[:, 1024 * c:1024 * (c + 1)], io["l1_wout"][128 * c:128 * (c + 1), :], sem="dCW")
    ge.dma(WQB[:, :], io["l1_wqb"], sem="dCW")
    ge.dma(WKVB_K[:, :], io["l1_wkvb_k"], sem="dCW")
    ge.dma(WKVB_V[:, :], io["l1_wkvb_v"], sem="dCW")
    ge.dma(CS_C[:, :], io["c_ropeC"], sem="dCW")
    wload = ge.dma(CS_S[:, :], io["c_ropeS"], sem="dCW")
    bv = load_bias(p, B, io["l1_bout"], 1024, BO, banks[6])

    for tc8 in range(8):
        t0 = 512 * tc8
        p.wait("gpsimd", "T", p.now("T"))
        xv = load_xchunk(p, B, io["r2T"], XTC, t0, 512, "gpsimd", "dX")
        p.wait("tensor", "dX", xv)
        p.wait("tensor", "dCW", wload)
        bk = banks[tc8 % 2]
        bk.begin_write(p)
        for c in range(8):
            te.matmul(bk.ap[0:32, 0:512], WKVA[:, 32 * c:32 * (c + 1)],
                      XTC[:, 512 * c:512 * (c + 1)], start=(c == 0), stop=(c == 7))
        bk.end_write(p, te)
        bk.rd(p, se)
        se.activation(KVU[0:32, t0:t0 + 512], bk.ap[0:32, 0:512], AF.Identity)
        bk.done_read(p, se)
    sv = p.now("S")
    p.wait("tensor", "S", sv)
    for g in range(4):
        rbase = (64 * g) % 128
        ch = g // 2
        for buf, wv in ((KR, WKVB_K), (VR, WKVB_V)):
            for tc8 in range(8):
                t0 = 512 * tc8
                bk = banks[tc8 % 2]
                bk.begin_write(p)
                te.matmul(bk.ap[0:64, 0:512], wv[:, 64 * g:64 * (g + 1)],
                          KVU[0:32, t0:t0 + 512], start=True, stop=True)
                bk.end_write(p, te)
                bk.rd(p, ve)
                ve.tensor_copy(buf[rbase:rbase + 64, 4096 * ch + t0:4096 * ch + t0 + 512],
                               bk.ap[0:64, 0:512])
                bk.done_read(p, ve)

    kv_done = p.now("V")
    p.wait("tensor", "V", kv_done)
    for ch in range(2):
        for tc8 in range(8):
            t0 = 512 * tc8
            bk = banks[2 + tc8 % 2]
            bk.begin_write(p)
            te.matmul(bk.ap[:, 0:512], fixed["pm"][:, :],
                      KR[:, 4096 * ch + t0:4096 * ch + t0 + 512], start=True, stop=True)
            bk.end_write(p, te)
            ve.tensor_mul(TMP1[:, :], KR[:, 4096 * ch + t0:4096 * ch + t0 + 512],
                          CS_C[:, t0:t0 + 512])
            bk.rd(p, ve)
            ve.tensor_mul(TMP2[:, :], bk.ap[:, 0:512], CS_S[:, t0:t0 + 512])
            bk.done_read(p, ve)
            ve.tensor_add(KR[:, 4096 * ch + t0:4096 * ch + t0 + 512], TMP1[:, :], TMP2[:, :])
    ve.memset(VT[:, :], 1.0)
    vt_init = p.now("V")
    p.wait("tensor", "V", vt_init)
    for ch in range(2):
        for kc4 in range(8):
            bankb.begin_write(p)
            for q in range(4):
                kc = 4 * kc4 + q
                te.transpose(bankb.ap[:, 256 * q:256 * q + 128],
                             VR[0:128, 4096 * ch + 128 * kc:4096 * ch + 128 * (kc + 1)],
                             B["fixed"]["ident_b"][0:128, 0:128])
            bankb.end_write(p, te)
            bankb.rd(p, ve)
            for q in range(4):
                kc = 4 * kc4 + q
                for half in range(2):
                    g = 2 * ch + half
                    ve.tensor_copy(VT[:, 65 * (32 * g + kc):65 * (32 * g + kc) + 64],
                                   bankb.ap[:, 256 * q + 64 * half:256 * q + 64 * half + 64])
            bankb.done_read(p, ve)

    st = dict(acc_st=0, pc0=0, pc1=0)
    for tcq in range(8):
        t0 = 512 * tcq
        p.wait("gpsimd", "T", p.now("T"))
        xv = load_xchunk(p, B, io["r2T"], XTC, t0, 512, "gpsimd", "dX")
        p.wait("tensor", "dX", xv)
        bk = banks[0]
        bk.begin_write(p)
        for c in range(8):
            te.matmul(bk.ap[0:64, 0:512], WQA[:, 64 * c:64 * (c + 1)],
                      XTC[:, 512 * c:512 * (c + 1)], start=(c == 0), stop=(c == 7))
        bk.end_write(p, te)
        bk.rd(p, se)
        p.wait("scalar", "T", p.now("T"))
        se.activation(USB[0:64, :], bk.ap[0:64, 0:512], AF.Identity)
        bk.done_read(p, se)
        uv = p.now("S")
        p.wait("tensor", "S", uv)
        p.wait("vector", "T", p.now("T"))
        p.wait("scalar", "V", p.now("V"))
        for jc in range(8):
            bq = banks[1 + jc % 2]
            bq.begin_write(p)
            te.matmul(bq.ap[:, 0:512], WQB[:, 128 * jc:128 * (jc + 1)],
                      USB[0:64, :], start=True, stop=True)
            bq.end_write(p, te)
            bq.rd(p, se)
            se.activation(QH[:, 512 * jc:512 * (jc + 1)], bq.ap[:, 0:512], AF.Identity)
            qsv = p.now("S")
            p.wait("tensor", "S", qsv)
            bs = banks[3 + jc % 2]
            bs.begin_write(p)
            te.matmul(bs.ap[:, 0:512], fixed["pm"][:, :],
                      QH[:, 512 * jc:512 * (jc + 1)], start=True, stop=True)
            bs.end_write(p, te)
            p.wait("vector", "S", qsv)
            ve.tensor_mul(TMP1[:, :], QH[:, 512 * jc:512 * (jc + 1)], CS_C[:, t0:t0 + 512])
            bs.rd(p, ve)
            ve.tensor_mul(TMP2[:, :], bs.ap[:, 0:512], CS_S[:, t0:t0 + 512])
            bs.done_read(p, ve)
            bq.done_read(p, se)
            ve.tensor_add(QH[:, 512 * jc:512 * (jc + 1)], TMP1[:, :], TMP2[:, :])
            ve.tensor_copy(QHs[0:64, 512 * jc:512 * (jc + 1)],
                           QH[64:128, 512 * jc:512 * (jc + 1)])
            ve.tensor_copy(QHs[64:128, 512 * jc:512 * (jc + 1)],
                           QH[0:64, 512 * jc:512 * (jc + 1)])
        qv = p.now("V")
        p.wait("tensor", "V", qv)
        for h in range(16):
            g = h // 4
            kb = (64 * g) % 128
            qb = (64 * h) % 128
            qbuf = QH if kb == qb else QHs
            nk = 4 * tcq + 4
            bo = banks[2 + h % 2]
            bo.begin_write(p)
            for j in range(nk):
                qoff = max(0, 128 * (j - 4 * tcq))
                bs = banks[h % 2]
                bs.begin_write(p)
                te.matmul(bs.ap[:, qoff:512],
                          KR[kb:kb + 64, 4096 * (g // 2) + 128 * j:4096 * (g // 2) + 128 * (j + 1)],
                          qbuf[kb:kb + 64, 512 * (h // 2) + qoff:512 * (h // 2) + 512],
                          start=True, stop=True)
                bs.end_write(p, te)
                bs.rd(p, se)
                p.wait("scalar", "T", st["pc%d" % (j % 2)])
                se.activation(PC[j % 2][:, qoff:512], bs.ap[:, qoff:512],
                              AF.Exp, scale=0.125)
                ev = bs.done_read(p, se)
                if j >= 4 * tcq:
                    p.wait("vector", "S", ev)
                    ve.tensor_mul(PC[j % 2][:, qoff:qoff + 128],
                                  PC[j % 2][:, qoff:qoff + 128], fixed["tri_le"][:, :])
                    ev2 = p.now("V")
                    p.wait("tensor", "V", ev2)
                else:
                    p.wait("tensor", "S", ev)
                te.matmul(bo.ap[0:65, qoff:512],
                          VT[:, 65 * (32 * g + j):65 * (32 * g + j) + 65],
                          PC[j % 2][:, qoff:512], start=(j == 0), stop=(j == nk - 1))
                st["pc%d" % (j % 2)] = p.now("T")
            bo.end_write(p, te)
            bo.rd(p, ve)
            p.wait("vector", "T", p.now("T"))
            ve.reciprocal(RZ[0:1, :], bo.ap[64:65, 0:512])
            ve.tensor_copy(OT[qb:qb + 64, :], bo.ap[0:64, 0:512])
            ve.tensor_copy(RZB[0:1, :], RZ[0:1, :])
            rv = bo.done_read(p, ve)
            p.wait("tensor", "V", rv)
            bz = banks[4]
            bz.begin_write(p)
            te.matmul(bz.ap[0:64, 0:512], fixed["ones_b"][0:1, 0:64],
                      RZB[0:1, :], start=True, stop=True)
            bz.end_write(p, te)
            bz.rd(p, ve)
            ve.tensor_mul(OBUF[qb:qb + 64, 512 * (h // 2):512 * (h // 2) + 512],
                          OT[qb:qb + 64, :], bz.ap[0:64, 0:512])
            bz.done_read(p, ve)
        obv = p.now("V")
        p.wait("tensor", "V", obv)
        for w in range(2):
            tcol = t0 + 256 * w
            p.wait("sync", "V", p.now("V"))
            lv = load_xchunk(p, B, io["r2T"], R2C, tcol, 256, "sync", "dR")
            accs = []
            for j in range(8):
                bk = banks[5 + j % 2]
                bk.begin_write(p)
                for c in range(8):
                    te.matmul(bk.ap[:, 0:256],
                              WOUT[:, 1024 * c + 128 * j:1024 * c + 128 * (j + 1)],
                              OBUF[:, 512 * c + 256 * w:512 * c + 256 * (w + 1)],
                              start=(c == 0), stop=(c == 7))
                bk.end_write(p, te)
                bk.rd(p, ve)
                if j == 0:
                    p.wait("vector", "dR", lv)
                    p.wait("vector", "dS", st["acc_st"])
                ve.scalar_tensor_tensor(ACC[:, 256 * j:256 * (j + 1)], bk.ap[:, 0:256],
                                        BO[:, j:j + 1], R2C[:, 256 * j:256 * (j + 1)],
                                        OP.add, OP.add)
                bk.done_read(p, ve)
                accs.append(ACC[:, 256 * j:256 * (j + 1)])
            vv, dv = emit_ln(p, B, accs, 256, io["r3T"], tcol)
            st["acc_st"] = dv


def emit_phase_d(p, B):
    io, fixed, banks, bankb, PS = B["io"], B["fixed"], B["banks"], B["bankb"], B["PS"]
    WA, XA, FA, SM = B["WA"], B["XA"], B["FA"], B["SM"]
    te, ve, se, sy, ge = (p.e("tensor"), p.e("vector"), p.e("scalar"),
                          p.e("sync"), p.e("gpsimd"))

    WQKV = WA.alloc((128, 8 * 1536), BF16)
    WO = WA.alloc((128, 4 * 1024), BF16)
    CQ = WA.alloc((128, T), BF16)
    SQ = WA.alloc((128, T), BF16)
    CK = WA.alloc((128, T), BF16)
    SK = WA.alloc((128, T), BF16)
    SH = XA.alloc((128, 8 * 512), BF16)
    QHAT = XA.alloc((128, 4 * 512), BF16)
    KS = [XA.alloc((128, 4 * 512), BF16) for _ in range(2)]
    VTS = [XA.alloc((128, 8 * 4 * 65), BF16) for _ in range(2)]
    OB = XA.alloc((128, 4 * 512), BF16)
    PC = [XA.alloc((128, 512), BF16) for _ in range(2)]
    SQS = XA.alloc((128, 512), BF16)
    VS = XA.alloc((128, 512), BF16)
    NMB = FA.alloc((1, 512), BF16)
    QSC = SM.alloc((128, 2), F32)
    RZ = FA.alloc((1, 512), F32)
    RZB = FA.alloc((1, 512), BF16)
    F1 = FA.alloc((128, 4096), F32)
    F2 = FA.alloc((128, 4096), F32)
    OT = FA.alloc((128, 512), F32)
    RM = FA.alloc((128, 8), F32)
    SHB = XA.alloc((128, 8 * 256), BF16)
    GL = XA.alloc((128, 22 * 256), BF16)
    GT = XA.alloc((128, 256), BF16)
    W1A = [XA.alloc((128, 8 * 128), BF16) for _ in range(2)]
    W1G = [XA.alloc((128, 8 * 128), BF16) for _ in range(2)]
    W2C = [XA.alloc((128, 22 * 128), BF16) for _ in range(2)]
    B1A = SM.alloc((128, 22), F32)
    B1G = SM.alloc((128, 22), F32)
    B2 = SM.alloc((128, 8), F32)
    LNG = SM.alloc((128, 8), F32)
    LNB = SM.alloc((128, 8), F32)

    ve.memset(VTS[0][:, :], 1.0)
    ve.memset(VTS[1][:, :], 1.0)

    def load_shifted(src, buf, t0, W, eng, sem, fp32=False):
        e = p.e(eng)
        bufr = buf.rearrange("p (c w) -> p c w", w=W)
        srcr = src.rearrange("(c p) t -> p c t", p=128)
        v = e.dma(bufr[:, 0:4, :], srcr[:, 0:4, t0:t0 + W], sem=sem)
        if t0 > 0:
            v = e.dma(bufr[:, 4:8, :], srcr[:, 4:8, t0 - 1:t0 + W - 1], sem=sem)
        else:
            v = e.dma(bufr[:, 4:8, 1:W], srcr[:, 4:8, 0:W - 1], sem=sem)
        return v

    def stage1(l, src_name):
        src = io[src_name]
        for c in range(8):
            ge.dma(WQKV[:, 1536 * c:1536 * (c + 1)],
                   io["lf_wqkv"][l, 128 * c:128 * (c + 1), :], sem="dCW")
        for c in range(4):
            ge.dma(WO[:, 1024 * c:1024 * (c + 1)],
                   io["lf_wo"][l, 128 * c:128 * (c + 1), :], sem="dCW")
        ge.dma(CQ[:, :], io["c_xposCQ"], sem="dCW")
        ge.dma(SQ[:, :], io["c_xposSQ"], sem="dCW")
        ge.dma(CK[:, :], io["c_xposCK"], sem="dCW")
        wl = ge.dma(SK[:, :], io["c_xposSK"], sem="dCW")
        sy.dma(QSC[0:64, 0:1], io["lf_qs"][l], sem="dR")
        sy.dma(QSC[64:128, 0:1], io["lf_qs"][l], sem="dR")
        sy.dma(QSC[0:64, 1:2], io["lf_ks"][l], sem="dR")
        qscv = sy.dma(QSC[64:128, 1:2], io["lf_ks"][l], sem="dR")
        p.wait("vector", "dR", qscv)

        for tc in range(8):
            t0 = 512 * tc
            slot = tc % 2
            p.wait("gpsimd", "T", p.now("T"))
            p.wait("gpsimd", "V", p.now("V"))
            xv = load_shifted(src, SH, t0, 512, "gpsimd", "dX")
            if t0 == 0:
                p.wait("vector", "dX", xv)
                for c in range(4, 8):
                    ve.memset(SH[:, c * 512:c * 512 + 1], 0.0)
            p.wait("tensor", "dX", xv)
            p.wait("tensor", "dCW", wl)
            if t0 == 0:
                p.wait("tensor", "V", p.now("V"))
            p.wait("scalar", "T", p.now("T"))
            p.wait("scalar", "V", p.now("V"))
            for part, dst in ((0, QHAT), (1, KS[slot]), (2, None)):
                for j in range(4):
                    bk = banks[j % 2]
                    bk.begin_write(p)
                    for c in range(8):
                        te.matmul(bk.ap[:, 0:512],
                                  WQKV[:, 1536 * c + 512 * part + 128 * j:
                                       1536 * c + 512 * part + 128 * (j + 1)],
                                  SH[:, 512 * c:512 * (c + 1)],
                                  start=(c == 0), stop=(c == 7))
                    bk.end_write(p, te)
                    if part < 2:
                        bk.rd(p, se)
                        se.activation(dst[:, 512 * j:512 * (j + 1)], bk.ap[:, 0:512],
                                      AF.Identity)
                        bk.done_read(p, se)
                    else:
                        bk.rd(p, se)
                        p.wait("scalar", "T", st["vs_free"])
                        se.activation(VS[:, :], bk.ap[:, 0:512], AF.Identity)
                        bk.done_read(p, se)
                        vsv = p.now("S")
                        p.wait("tensor", "S", vsv)
                        bankb.begin_write(p)
                        for kc in range(4):
                            te.transpose(
                                bankb.ap[:, 256 * kc:256 * kc + 128],
                                VS[0:128, 128 * kc:128 * (kc + 1)],
                                B["fixed"]["ident_b"][0:128, 0:128])
                        bankb.end_write(p, te)
                        st["vs_free"] = bankb.w
                        bankb.rd(p, ve)
                        for kc in range(4):
                            for half in range(2):
                                h = 2 * j + half
                                ve.tensor_copy(
                                    VTS[slot][:, 65 * (4 * h + kc):65 * (4 * h + kc) + 64],
                                    bankb.ap[:, 256 * kc + 64 * half:256 * kc + 64 * half + 64])
                        bankb.done_read(p, ve)
            qkv_s = p.now("S")
            p.wait("scalar", "S", qkv_s)
            for buf, scol, ct, stt_ in ((QHAT, 0, CQ, SQ), (KS[slot], 1, CK, SK)):
                for j in range(4):
                    sv0 = p.now("S")
                    p.wait("scalar", "V", p.now("V"))
                    se.activation(SQS[:, :], buf[:, 512 * j:512 * (j + 1)], AF.Square)
                    sv = p.now("S")
                    p.wait("tensor", "S", sv)
                    for half in range(2):
                        rbase = 64 * half
                        bn = banks[4 + half]
                        bn.begin_write(p)
                        te.matmul(bn.ap[0:1, 0:512],
                                  fixed["ones_b"][rbase:rbase + 64, 0:1],
                                  SQS[rbase:rbase + 64, :], start=True, stop=True)
                        bn.end_write(p, te)
                        bn.rd(p, ve)
                        ve.tensor_scalar_max(RZ[0:1, :], bn.ap[0:1, 0:512], 1e-24)
                        bn.done_read(p, ve)
                        vv = p.now("V")
                        p.wait("scalar", "V", vv)
                        se.activation(RZ[0:1, :], RZ[0:1, :], AF.Ln)
                        p.ss("scalar")
                        se.activation(RZ[0:1, :], RZ[0:1, :], AF.Exp, scale=-0.5)
                        sv2 = p.now("S")
                        p.wait("vector", "S", sv2)
                        ve.tensor_copy(RZB[0:1, :], RZ[0:1, :])
                        vv2 = p.now("V")
                        p.wait("tensor", "V", vv2)
                        bb2 = banks[6]
                        bb2.begin_write(p)
                        te.matmul(bb2.ap[0:64, 0:512], fixed["ones_b"][0:1, 0:64],
                                  RZB[0:1, :], start=True, stop=True)
                        bb2.end_write(p, te)
                        bb2.rd(p, ve)
                        ve.scalar_tensor_tensor(
                            buf[rbase:rbase + 64, 512 * j:512 * (j + 1)],
                            buf[rbase:rbase + 64, 512 * j:512 * (j + 1)],
                            QSC[rbase:rbase + 64, scol:scol + 1],
                            bb2.ap[0:64, 0:512], OP.mult, OP.mult)
                        bb2.done_read(p, ve)
                    vv = p.now("V")
                    p.wait("tensor", "V", vv)
                    bsf = banks[3]
                    bsf.begin_write(p)
                    te.matmul(bsf.ap[:, 0:512], fixed["pm"][:, :],
                              buf[:, 512 * j:512 * (j + 1)], start=True, stop=True)
                    bsf.end_write(p, te)
                    ve.tensor_mul(F1[:, 3584:4096] if False else OT[:, :],
                                  buf[:, 512 * j:512 * (j + 1)], ct[:, t0:t0 + 512])
                    bsf.rd(p, ve)
                    ve.scalar_tensor_tensor(buf[:, 512 * j:512 * (j + 1)],
                                            bsf.ap[:, 0:512], 1.0,
                                            stt_[:, t0:t0 + 512], OP.mult, OP.mult)
                    p.ss("vector")
                    ve.tensor_add(buf[:, 512 * j:512 * (j + 1)],
                                  buf[:, 512 * j:512 * (j + 1)], OT[:, :])
                    bsf.done_read(p, ve)
            qv = p.now("V")
            p.wait("tensor", "V", qv)
            for h in range(8):
                rbase = 64 * (h % 2)
                hj = h // 2
                chunks = [("own", 0)]
                if tc > 0:
                    chunks += [("prev", r) for r in range(4)]
                chunks += [("own", r) for r in range(1, 4)]
                bo = banks[2]
                bo.begin_write(p)
                for qb in range(4):
                    w_own = 128 * (qb + 1)
                    b5, b6 = banks[5], banks[6]
                    b6.begin_write(p)
                    te.matmul(b6.ap[:, 0:w_own],
                              QHAT[rbase:rbase + 64, 512 * hj + 128 * qb:512 * hj + 128 * (qb + 1)],
                              KS[slot][rbase:rbase + 64, 512 * hj:512 * hj + w_own],
                              start=True, stop=True)
                    b6.end_write(p, te)
                    if tc > 0:
                        w_prev = 512 - 128 * qb
                        b5.begin_write(p)
                        te.matmul(b5.ap[:, 0:w_prev],
                                  QHAT[rbase:rbase + 64, 512 * hj + 128 * qb:512 * hj + 128 * (qb + 1)],
                                  KS[1 - slot][rbase:rbase + 64, 512 * hj + 128 * qb:512 * hj + 512],
                                  start=True, stop=True)
                        b5.end_write(p, te)
                        b5.rd(p, ve)
                        ve.reduce_max(RM[:, 4 + qb:5 + qb], b5.ap[:, 0:w_prev], axis=AX)
                        b5.done_read(p, ve)
                    b6.rd(p, ve)
                    ve.reduce_max(RM[:, qb:qb + 1], b6.ap[:, 0:w_own], axis=AX)
                    b6.done_read(p, ve)
                    if tc > 0:
                        ve.tensor_max(RM[:, qb:qb + 1], RM[:, qb:qb + 1], RM[:, 4 + qb:5 + qb])
                rmv = p.now("V")
                p.wait("tensor", "V", rmv)
                bnm = banks[5]
                bnm.begin_write(p)
                for qb in range(4):
                    te.transpose(bnm.ap[0:1, 128 * qb:128 * (qb + 1)],
                                 RM[:, qb:qb + 1], fixed["ident_f"][:, :])
                bnm.end_write(p, te)
                bnm.rd(p, se)
                se.activation(NMB[0:1, :], bnm.ap[0:1, 0:512], AF.Identity, scale=-1.0)
                bnm.done_read(p, se)
                nmv = p.now("S")
                p.wait("tensor", "S", nmv)
                first = True
                for kind, r in chunks:
                    ks_ = KS[1 - slot] if kind == "prev" else KS[slot]
                    vt_ = VTS[1 - slot] if kind == "prev" else VTS[slot]
                    if kind == "prev":
                        qlo, qhi = 0, 128 * (r + 1)
                    else:
                        qlo, qhi = 128 * r, 512
                    bs = banks[h % 2]
                    bs.begin_write(p)
                    te.matmul(bs.ap[:, qlo:qhi],
                              ks_[rbase:rbase + 64, 512 * hj + 128 * r:512 * hj + 128 * (r + 1)],
                              QHAT[rbase:rbase + 64, 512 * hj + qlo:512 * hj + qhi],
                              start=True, stop=False)
                    te.matmul(bs.ap[:, qlo:qhi], fixed["ones_b"][0:1, :],
                              NMB[0:1, qlo:qhi], start=False, stop=True)
                    bs.end_write(p, te)
                    bs.rd(p, se)
                    p.wait("scalar", "T", st["pc%d" % (r % 2)])
                    se.activation(PC[r % 2][:, qlo:qhi], bs.ap[:, qlo:qhi],
                                  AF.Exp, scale=8.0)
                    ev = bs.done_read(p, se)
                    p.wait("vector", "S", ev)
                    mask = fixed["tri_ge"] if kind == "prev" else fixed["tri_le"]
                    mb = 128 * r
                    ve.tensor_mul(PC[r % 2][:, mb:mb + 128], PC[r % 2][:, mb:mb + 128],
                                  mask[:, :])
                    ev2 = p.now("V")
                    p.wait("tensor", "V", ev2)
                    last = (kind == "own" and r == 3) if tc > 0 else False
                    last = (kind == "own" and r == 3)
                    te.matmul(bo.ap[0:65, qlo:qhi],
                              vt_[:, 65 * (4 * h + r):65 * (4 * h + r) + 65],
                              PC[r % 2][:, qlo:qhi], start=first, stop=last)
                    st["pc%d" % (r % 2)] = p.now("T")
                    first = False
                bo.end_write(p, te)
                bo.rd(p, ve)
                p.wait("vector", "T", p.now("T"))
                ve.reciprocal(RZ[0:1, :], bo.ap[64:65, 0:512])
                ve.tensor_copy(OT[rbase:rbase + 64, :], bo.ap[0:64, 0:512])
                ve.tensor_copy(RZB[0:1, :], RZ[0:1, :])
                rv = bo.done_read(p, ve)
                p.wait("tensor", "V", rv)
                bz = banks[6]
                bz.begin_write(p)
                te.matmul(bz.ap[0:64, 0:512], fixed["ones_b"][0:1, 0:64],
                          RZB[0:1, :], start=True, stop=True)
                bz.end_write(p, te)
                bz.rd(p, ve)
                ve.tensor_mul(OB[rbase:rbase + 64, 512 * hj:512 * hj + 512],
                              OT[rbase:rbase + 64, :], bz.ap[0:64, 0:512])
                bz.done_read(p, ve)
            obv = p.now("V")
            p.wait("sync", "V", p.now("V"))
            xpv = load_xchunk(p, B, src, F2, t0, 512, "sync", "dR")
            p.wait("tensor", "V", obv)
            for j in range(8):
                bk = banks[5 + j % 2]
                bk.begin_write(p)
                for c in range(4):
                    te.matmul(bk.ap[:, 0:512],
                              WO[:, 1024 * c + 128 * j:1024 * c + 128 * (j + 1)],
                              OB[:, 512 * c:512 * (c + 1)],
                              start=(c == 0), stop=(c == 3))
                bk.end_write(p, te)
                bk.rd(p, ve)
                if j == 0:
                    p.wait("vector", "dR", xpv)
                    p.wait("vector", "dS", st["xn_st"])
                ve.tensor_add(F1[:, 512 * j:512 * (j + 1)], F2[:, 512 * j:512 * (j + 1)],
                              bk.ap[:, 0:512])
                bk.done_read(p, ve)
            xnv = p.now("V")
            p.wait("sync", "V", xnv)
            for c in range(8):
                sy.dma(io["xiT"][128 * c:128 * (c + 1), t0:t0 + 512],
                       F1[:, 512 * c:512 * (c + 1)], sem="dS")
            st["xn_st"] = p.now("dS")

    def stage2(l, dst_name):
        final = dst_name is None
        load_bias(p, B, io["lf_b1"][l, 0:2730], 2730, B1A, banks[6])
        load_bias(p, B, io["lf_b1"][l, 2730:5460], 2730, B1G, banks[6])
        load_bias(p, B, io["lf_b2"][l], 1024, B2, banks[6])
        load_bias(p, B, io["lf_lng"][l], 1024, LNG, banks[6])
        bv = load_bias(p, B, io["lf_lnb"][l], 1024, LNB, banks[6])
        p.wait("vector", "V", bv)
        p.wait("scalar", "V", bv)

        def load_w1(hc, par):
            p.wait("gpsimd", "T", st["w1_free"][par])
            s = "dW%d" % par
            n = 128 if hc < 21 else 42
            v = ge.dma(W1A[par].rearrange("p (c f) -> p c f", f=128)[:, :, 0:n],
                       io["lf_w1"][l].rearrange("(c p) f -> p c f", p=128)[:, :, 128 * hc:128 * hc + n],
                       sem=s)
            v = ge.dma(W1G[par].rearrange("p (c f) -> p c f", f=128)[:, :, 0:n],
                       io["lf_w1"][l].rearrange("(c p) f -> p c f", p=128)[:, :, 2730 + 128 * hc:2730 + 128 * hc + n],
                       sem=s)
            return (s, v)

        def load_w2(j, par):
            p.wait("gpsimd", "T", st["w2_free"][par])
            s = "dWo%d" % par
            v = ge.dma(W2C[par][:, 0:21 * 128].rearrange("p (c d) -> p c d", d=128),
                       io["lf_w2"][l, 0:2688].rearrange("(c p) d -> p c d", p=128)[:, :, 128 * j:128 * (j + 1)],
                       sem=s)
            v = ge.dma(W2C[par][0:42, 21 * 128:22 * 128],
                       io["lf_w2"][l, 2688:2730, 128 * j:128 * (j + 1)], sem=s)
            return (s, v)

        for tc in range(8):
            t0 = 512 * tc
            for w in range(2):
                tcol = t0 + 256 * w
                p.wait("sync", "T", p.now("T"))
                p.wait("sync", "V", p.now("V"))
                shv = load_shifted(io["xiT"], F1[:, 0:2048], tcol, 256, "sync", "dR", fp32=True)
                if tcol == 0:
                    p.wait("vector", "dR", shv)
                    for c in range(4, 8):
                        ve.memset(F1[:, c * 256:c * 256 + 1], 0.0)
                p.wait("vector", "dR", shv)
                accs = [F1[:, 256 * c:256 * (c + 1)] for c in range(8)]
                shb = [SHB[:, 256 * c:256 * (c + 1)] for c in range(8)]
                lgl = [LNG[:, c:c + 1] for c in range(8)]
                lbl = [LNB[:, c:c + 1] for c in range(8)]
                emit_ln(p, B, accs, 256, None, 0, bf_out=shb, lg=lgl, lb=lbl)
                shbv = p.now("V")
                p.wait("vector", "T", p.now("T"))
                w1v = load_w1(0, 0)
                for hc in range(22):
                    par = hc % 2
                    if hc < 21:
                        nw1v = load_w1(hc + 1, 1 - par)
                    n = 128 if hc < 21 else 42
                    p.wait("tensor", w1v[0], w1v[1])
                    if hc == 0:
                        p.wait("tensor", "V", shbv)
                    ba = banks[0]
                    ba.begin_write(p)
                    for c in range(8):
                        te.matmul(ba.ap[0:n, 0:256], W1A[par][:, 128 * c:128 * c + n],
                                  SHB[:, 256 * c:256 * (c + 1)], start=(c == 0), stop=(c == 7))
                    ba.end_write(p, te)
                    bg = banks[1]
                    bg.begin_write(p)
                    for c in range(8):
                        te.matmul(bg.ap[0:n, 0:256], W1G[par][:, 128 * c:128 * c + n],
                                  SHB[:, 256 * c:256 * (c + 1)], start=(c == 0), stop=(c == 7))
                    bg.end_write(p, te)
                    st["w1_free"][par] = p.now("T")
                    bg.rd(p, se)
                    p.wait("scalar", "V", st["gt_free"])
                    se.activation(GT[0:n, :], bg.ap[0:n, 0:256], AF.Gelu,
                                  bias=B1G[0:n, hc:hc + 1])
                    bg.done_read(p, se)
                    gv = p.now("S")
                    p.wait("vector", "S", gv)
                    ba.rd(p, ve)
                    ve.scalar_tensor_tensor(GL[0:n, 256 * hc:256 * (hc + 1)],
                                            ba.ap[0:n, 0:256], B1A[0:n, hc:hc + 1],
                                            GT[0:n, :], OP.add, OP.mult)
                    ba.done_read(p, ve)
                    st["gt_free"] = p.now("V")
                    if hc < 21:
                        w1v = nw1v
                glv = p.now("V")
                p.wait("sync", "V", p.now("V"))
                p.wait("sync", "dS", st["xn_st"])
                xiv = load_xchunk(p, B, io["xiT"], F1[:, 2048:4096], tcol, 256, "sync", "dR")
                if final:
                    r3v = load_xchunk(p, B, io["r3T"], F2[:, 0:2048], tcol, 256, "sync", "dR")
                w2v = load_w2(0, 0)
                for j in range(8):
                    par = j % 2
                    if j < 7:
                        nw2v = load_w2(j + 1, 1 - par)
                    p.wait("tensor", w2v[0], w2v[1])
                    if j == 0:
                        p.wait("tensor", "V", glv)
                    bk = banks[2 + j % 2]
                    bk.begin_write(p)
                    for hc in range(22):
                        n = 128 if hc < 21 else 42
                        te.matmul(bk.ap[:, 0:256], W2C[par][0:n, 128 * hc:128 * (hc + 1)],
                                  GL[0:n, 256 * hc:256 * (hc + 1)],
                                  start=(hc == 0), stop=(hc == 21))
                    bk.end_write(p, te)
                    st["w2_free"][par] = p.now("T")
                    bk.rd(p, ve)
                    if j == 0:
                        p.wait("vector", "dR", xiv if not final else max(xiv, r3v))
                        p.wait("vector", "dS", st["xn_st"])
                    ve.scalar_tensor_tensor(F1[:, 2048 + 256 * j:2048 + 256 * (j + 1)],
                                            bk.ap[:, 0:256], B2[:, j:j + 1],
                                            F1[:, 2048 + 256 * j:2048 + 256 * (j + 1)],
                                            OP.add, OP.add)
                    bk.done_read(p, ve)
                    if j < 7:
                        w2v = nw2v
                if not final:
                    xnv = p.now("V")
                    p.wait("sync", "V", xnv)
                    for c in range(8):
                        sy.dma(io[dst_name][128 * c:128 * (c + 1), tcol:tcol + 256],
                               F1[:, 2048 + 256 * c:2048 + 256 * (c + 1)], sem="dS")
                    st["xn_st"] = p.now("dS")
                else:
                    for c in range(8):
                        ve.tensor_add(F1[:, 2048 + 256 * c:2048 + 256 * (c + 1)],
                                      F1[:, 2048 + 256 * c:2048 + 256 * (c + 1)],
                                      F2[:, 256 * c:256 * (c + 1)])
                    accs2 = [F1[:, 2048 + 256 * c:2048 + 256 * (c + 1)] for c in range(8)]
                    emit_ln(p, B, accs2, 256, None, 0)
                    lnv = p.now("V")
                    p.wait("tensor", "V", lnv)
                    for tt in range(2):
                        for c in range(8):
                            bt = banks[4 + c % 2]
                            bt.begin_write(p)
                            te.transpose(bt.ap[:, 0:128],
                                         F1[:, 2048 + 256 * c + 128 * tt:2048 + 256 * c + 128 * (tt + 1)],
                                         fixed["ident_f"][:, :])
                            bt.end_write(p, te)
                            bt.rd(p, ve)
                            ve.tensor_copy(F2[:, 2048 + 128 * c:2048 + 128 * (c + 1)],
                                           bt.ap[:, 0:128])
                            bt.done_read(p, ve)
                        ttv = p.now("V")
                        p.wait("sync", "V", ttv)
                        sy.dma(io["out"][tcol + 128 * tt:tcol + 128 * (tt + 1), :],
                               F2[:, 2048:3072], sem="dO")
                        p.wait("vector", "dO", p.now("dO"))

    st = dict(pc0=0, pc1=0, xn_st=0, w1_free=[0, 0], w2_free=[0, 0], gt_free=0, vs_free=0)
    stage1(0, "r3T")
    p.barrier()
    stage2(0, "r2T")
    p.barrier()
    stage1(1, "r2T")
    p.barrier()
    stage2(1, None)



FUSED = True


def kernel(**inputs):
    sh, xs = _host_prep(inputs)
    if FUSED:
        nc = build_program()
        in_maps = [dict(sh, xT=xs[i]) for i in range(NCORE)]
        res = run_bass_kernel_spmd(nc, in_maps, core_ids=list(range(NCORE)))
        out = np.stack([np.asarray(r["out"]) for r in res.results], axis=0)
        return out.astype(np.float32)
    cur = [None] * NCORE
    feed_of = {"a": None, "b": "r1T", "c": "r2T", "d": "r3T"}
    dump_of = {"a": "r1T", "b": "r2T", "c": "r3T", "d": "out"}
    for ph in ("a", "b", "c", "d"):
        feed = feed_of[ph]
        dump = dump_of[ph]
        nc = build_program(phases=(ph,), dump=(dump,),
                           feed=(feed,) if feed else ())
        in_maps = []
        for i in range(NCORE):
            m = dict(sh, xT=xs[i])
            if feed:
                m[feed] = cur[i]
            in_maps.append(m)
        res = run_bass_kernel_spmd(nc, in_maps, core_ids=list(range(NCORE)))
        cur = [np.asarray(r[dump]) for r in res.results]
    out = np.stack(cur, axis=0)
    return out.astype(np.float32)
